# version 08
# speedup vs baseline: 1.2686x; 1.0292x over previous
import numpy as np
import ml_dtypes
from contextlib import ExitStack

import concourse.bass as bass
import concourse.mybir as mybir
from concourse.bass_utils import run_bass_kernel_spmd

F32 = mybir.dt.float32
BF16 = mybir.dt.bfloat16
FP8 = mybir.dt.float8e4
I32 = mybir.dt.int32
ACT = mybir.ActivationFunctionType
ALU = mybir.AluOpType
AX = mybir.AxisListType
BF = ml_dtypes.bfloat16

D = 2048
NCH = 16
TB = 4096
SL = 1024
NH = 4
FI = 6144
NICH = 48
SCALE0 = float(1.0 / np.sqrt(np.sqrt(128.0)))
RND_C = 3.0 * 2.0 ** 22
EPS_LN = 1e-5

STOP_AFTER = 5
DEBUG3 = False
ENGS = ["tensor", "vector", "scalar", "sync", "gpsimd"]


class Sem:
    __slots__ = ("h", "n")

    def __init__(self, stack, nc, name):
        self.h = stack.enter_context(nc.semaphore(name))
        self.n = 0


class Prog:

    AUTODRAIN = ("vector", "scalar")

    def __init__(self):
        self.ops = {k: [] for k in ENGS}
        self._dirty = {k: False for k in ENGS}

    def wait(self, eng, sem, n=None):
        m = sem.n if n is None else n
        if m <= 0:
            return
        self.ops[eng].append(lambda e, s=sem.h, mm=m: e.wait_ge(s, mm))

    def do(self, eng, fn, inc=None, k=1):
        if eng in self.AUTODRAIN:
            if self._dirty[eng]:
                self.ops[eng].append(lambda e: e.drain())
            self._dirty[eng] = True
        if inc is not None:
            inc.n += k

            def run(e, f=fn, s=inc.h, kk=k):
                ins = f(e)
                ins.then_inc(s, kk)
            self.ops[eng].append(run)
        else:
            self.ops[eng].append(fn)

    def bump(self, eng, sem, k=1):
        sem.n += k
        self.ops[eng].append(lambda e, s=sem.h, kk=k, n=sem.n:
                             e.sem_inc(s, kk))

    def emit(self, block):
        for eng in ENGS:
            lst = self.ops[eng]
            if not lst:
                continue

            def body(e, lst=lst):
                for fn in lst:
                    fn(e)
            getattr(block, eng)(body)


def build_program(stop_after=STOP_AFTER, debug3=None):
    if debug3 is None:
        debug3 = DEBUG3
    nc = bass.Bass("TRN2", target_bir_lowering=False, debug=False)

    def din(name, shape, dtype=F32):
        return nc.dram_tensor(name, shape, dtype, kind="ExternalInput").ap()

    xtf = din("xtf", [8, 128, NCH, 512])
    xown = din("xown", [128, NCH, SL])
    wq0t = din("wq0t", [128, NCH, 512])
    wk0t = din("wk0t", [128, NCH, 512])
    wv0t = din("wv0t", [128, NCH, 512])
    b0 = din("b0", [128, 12])
    bv0b = din("bv0b", [128, 512])
    wo0t = din("wo0t", [NCH, 128, NCH, 128])
    bo0 = din("bo0", [128, NCH])
    fk0 = din("fk0", [NCH, 128, D])
    fb0 = din("fb0", [128, NCH])
    masks0 = din("masks0", [128, 4 * 512], BF16)
    masks1 = din("masks1", [128, 4 * 512], BF16)
    ropeC = din("ropeC", [128, SL])
    ropeS = din("ropeS", [128, SL])
    wq1t = din("wq1t", [NCH, 128, NCH, 128])
    wk1t = din("wk1t", [NCH, 128, NCH, 128])
    wg1t = din("wg1t", [NCH, 128, NCH, 128])
    wv1t = din("wv1t", [4, 128, NCH, 512])
    b1qkg = din("b1qkg", [128, 48])
    bv1b = din("bv1b", [128, D])
    wo1t = din("wo1t", [NCH, 128, NCH, 128])
    bo1 = din("bo1", [128, NCH])
    gw1t = din("gw1t", [NICH, 128, NCH, 128])
    uw1t = din("uw1t", [NICH, 128, NCH, 128])
    dw1t = din("dw1t", [NCH, 128, NICH, 128])
    gsc = din("gsc", [128, NICH])
    usc = din("usc", [128, NICH])
    dsc = din("dsc", [128, NCH])
    pidx = din("pidx", [1, 1], I32)
    pm = din("pm", [128, 4])
    eyein = din("eyein", [128, 128], BF16)
    onesrin = din("onesrin", [1, 128], BF16)

    y = nc.dram_tensor("y", [D, SL], F32, kind="ExternalOutput").ap()
    if debug3:
        dbgA = nc.dram_tensor("dbgA", [128, NCH * 512], F32,
                              kind="ExternalOutput").ap()
        dbgB = nc.dram_tensor("dbgB", [128, NCH * 512], BF16,
                              kind="ExternalOutput").ap()
        dbgC = nc.dram_tensor("dbgC", [128, 4 * 512], F32,
                              kind="ExternalOutput").ap()
        dbgD = nc.dram_tensor("dbgD", [128, 512 + 32], F32,
                              kind="ExternalOutput").ap()

    cin0 = nc.dram_tensor("cin0", [NH * 128, TB], BF16, kind="Internal").ap()
    cout0 = nc.dram_tensor("cout0", [4 * NH * 128, TB], BF16,
                           kind="Internal").ap()
    rkq = nc.dram_tensor("rkq", [NCH, 128, NCH, 128], BF16, kind="Internal").ap()
    strm_d = nc.dram_tensor("strm_d", [128, NCH, SL], F32, kind="Internal").ap()
    q1d = nc.dram_tensor("q1d", [4, 128, 4, SL], BF16, kind="Internal").ap()
    k1d = nc.dram_tensor("k1d", [4, 128, 4, SL], BF16, kind="Internal").ap()
    g1d = nc.dram_tensor("g1d", [4, 128, 4, SL], BF16, kind="Internal").ap()
    cin1 = nc.dram_tensor("cin1", [NCH * 128, 132], F32, kind="Internal").ap()
    cout1 = nc.dram_tensor("cout1", [4 * NCH * 128, 132], F32,
                           kind="Internal").ap()
    groups = [[0, 1, 2, 3], [4, 5, 6, 7]]

    root = ExitStack()

    def sbuf(st, name, shape, dtype=F32):
        return st.enter_context(nc.sbuf_tensor(name, shape, dtype))

    def psum(st, name, shape, dtype=F32):
        return st.enter_context(nc.psum_tensor(name, shape, dtype))

    with root:
        b0sb = sbuf(root, "b0sb", [128, 12])
        b0scl = sbuf(root, "b0scl", [128, 8])
        bo0sb = sbuf(root, "bo0sb", [128, NCH])
        fb0sb = sbuf(root, "fb0sb", [128, NCH])
        b1sb = sbuf(root, "b1sb", [128, 48])
        bo1sb = sbuf(root, "bo1sb", [128, NCH])
        gscsb = sbuf(root, "gscsb", [128, NICH])
        uscsb = sbuf(root, "uscsb", [128, NICH])
        dscsb = sbuf(root, "dscsb", [128, NCH])
        pmsb = sbuf(root, "pmsb", [128, 4])
        pidxsb = sbuf(root, "pidxsb", [1, 1], I32)
        onesc = sbuf(root, "onesc", [128, 1], BF16)
        onesr = sbuf(root, "onesr", [1, 128], BF16)
        onesrF = sbuf(root, "onesrF", [1, 128])
        eyeb = sbuf(root, "eyeb", [128, 128], BF16)
        rows = sbuf(root, "rows", [1, 8 * 512])
        rowb = sbuf(root, "rowb", [1, 8 * 512], BF16)
        colsA = sbuf(root, "colsA", [128, 32])
        colsB = sbuf(root, "colsB", [128, 32], BF16)
        s123 = ExitStack()
        rkrow = sbuf(s123, "rkrow", [1, D])
        skcol = sbuf(s123, "skcol", [128, NCH])
        bqq = sbuf(s123, "bqq", [128, NCH])
        bv0sb = sbuf(s123, "bv0sb", [128, 512])

        sem_names = ["ld", "gd", "xf", "fkl", "fkv", "tq", "sq", "tk", "sk2",
                     "tv", "vv", "tsc", "se", "vm", "ta", "vr", "tb", "vo",
                     "ow0", "ow1", "cc", "st", "tx", "vx", "sx2", "gx",
                     "gw", "sld", "go", "wo0", "wo1", "rkw", "rk0", "rk1",
                     "phT", "phV", "phS", "ty", "sdw", "g40", "g41", "g4v",
                     "t4", "t4v", "o40", "o41", "s40", "s41", "s42", "s43",
                     "cc2", "g50", "g51", "t5", "f50", "f51", "d50", "d51",
                     "sf5", "t4t", "v4", "v4b", "s4e", "s4p"]
        S = {nm: Sem(root, nc, nm) for nm in sem_names}
        ld, gd, xf, fkl, fkv = (S[n] for n in ["ld", "gd", "xf", "fkl", "fkv"])
        tq, sq, tk, sk2, tv, vv = (S[n] for n in
                                   ["tq", "sq", "tk", "sk2", "tv", "vv"])
        tsc, se, vm, ta = (S[n] for n in ["tsc", "se", "vm", "ta"])
        vr, tb, vo, cc = (S[n] for n in ["vr", "tb", "vo", "cc"])
        ow = [S["ow0"], S["ow1"]]
        st, tx, vx, sx2, gx = (S[n] for n in ["st", "tx", "vx", "sx2", "gx"])
        gw, sld, go = S["gw"], S["sld"], S["go"]
        wo_s = [S["wo0"], S["wo1"]]
        rkw = S["rkw"]
        rk_s = [S["rk0"], S["rk1"]]
        ty = S["ty"]

        block = root.enter_context(nc.Block())
        P = Prog()

        def phase_barrier():
            for eng, psem in (("tensor", S["phT"]), ("vector", S["phV"]),
                              ("scalar", S["phS"])):
                P.do(eng, lambda e: e.drain(), psem)
            for eng in ENGS:
                for sem in S.values():
                    P.wait(eng, sem)

        for dst, src in [(b0sb, b0), (bo0sb, bo0), (fb0sb, fb0),
                         (b1sb, b1qkg), (bo1sb, bo1), (gscsb, gsc),
                         (uscsb, usc), (dscsb, dsc), (pmsb, pm),
                         (pidxsb, pidx), (eyeb, eyein), (onesr, onesrin),
                         (bv0sb, bv0b)]:
            P.do("sync", lambda e, d=dst, s=src: e.dma_start(out=d[:],
                                                             in_=s[:]), ld, 16)
        P.do("vector", lambda e: e.memset(onesc[:], 1.0))
        P.do("vector", lambda e: e.memset(onesrF[:], 1.0))
        _b0scl_todo = True

        s1 = ExitStack()
        w0sb = {nm: sbuf(s1, f"w0{nm}", [128, NCH * 512], BF16)
                for nm in ["q", "k", "v"]}
        xts = sbuf(s1, "xts", [128, NCH * 512], BF16)
        qb = sbuf(s1, "qb", [128, NH * 512], BF16)
        kb = sbuf(s1, "kb", [128, NH * TB], BF16)
        vb = sbuf(s1, "vb", [128, 32 * 512], BF16)
        pb = sbuf(s1, "pb", [128, 2 * 512], BF16)
        m0 = sbuf(s1, "m0", [128, 4 * 512], BF16)
        osb = sbuf(s1, "osb", [128, 2 * 512], BF16)
        rdsb = sbuf(s1, "rdsb", [128, 512])
        fkbuf = sbuf(s1, "fkbuf", [128, D])
        fkmax = sbuf(s1, "fkmax", [128, D])
        fkscb = sbuf(s1, "fkscb", [128, 256], BF16)
        fksc = sbuf(s1, "fksc", [128, 256])
        psQ = psum(s1, "psQ", [128, 512])
        psK = psum(s1, "psK", [128, 512])
        psV = psQ
        psS = psum(s1, "psS", [128, 512])
        psS2 = psum(s1, "psS2", [128, 512])
        psAV = psum(s1, "psAV", [128, 512])
        psBC = psum(s1, "psBC", [128, 512])
        psDen = psum(s1, "psDen", [1, 512])
        psT = psum(s1, "psT", [128, 128], BF16)

        P.do("sync", lambda e: e.dma_start(out=m0[:], in_=masks0[:]), ld, 16)
        prologue_ld2 = ld.n

        for nm, src in [("q", wq0t), ("k", wk0t), ("v", wv0t)]:
            P.do("gpsimd", lambda e, nm=nm, src=src: e.dma_start(
                out=w0sb[nm].rearrange("p (c f) -> p c f", c=NCH),
                in_=src[:]), gw, 16)
        P.wait("vector", ld, prologue_ld2)
        P.do("vector", lambda e: e.tensor_scalar_mul(
            out=b0scl[:], in0=b0sb[:, 0:8], scalar1=SCALE0), vx)
        for tt in range(8):
            if tt >= 1:
                P.wait("gpsimd", tq, NH * tt)
                P.wait("gpsimd", tk, NH * tt)
                P.wait("gpsimd", tv, 4 * tt)
            P.do("gpsimd", lambda e, tt=tt: e.dma_start(
                out=xts.rearrange("p (c t) -> p c t", c=NCH),
                in_=xtf[tt]), gx, 16)

        def fk_load_step(i):
            if i >= NCH:
                return
            if i >= 1:
                P.wait("sync", fkv, i)
            P.do("sync", lambda e, i=i: e.dma_start(out=fkbuf[:],
                                                    in_=fk0[i]), fkl, 16)

        fk_load_step(0)
        fk_load_step(1)

        def fk_fold_step(i):
            P.wait("vector", fkl, (i + 1) * 16)
            if i == 0:
                P.do("vector", lambda e: e.scalar_tensor_tensor(
                    out=fkmax[:], in0=fkbuf[:], scalar=-1.0,
                    in1=fkbuf[:], op0=ALU.mult, op1=ALU.max), fkv)
            else:
                P.do("vector", lambda e: e.scalar_tensor_tensor(
                    out=fkbuf[:], in0=fkbuf[:], scalar=-1.0,
                    in1=fkbuf[:], op0=ALU.mult, op1=ALU.max))
                P.do("vector", lambda e: e.tensor_max(
                    out=fkmax[:], in0=fkmax[:], in1=fkbuf[:]), fkv)

        P.wait("tensor", vx, vx.n)
        P.wait("scalar", vx, vx.n)
        P.wait("tensor", gw, 48)
        kt_glob = 0
        o_idx = 0
        for tt in range(8):
            P.wait("tensor", gx, (tt + 1) * 16)
            xv = xts.rearrange("p (c t) -> p c t", c=NCH)
            for h in range(NH):
                for nm, pq, tsem, ssem, bcol in (("q", psQ, tq, sq, h),
                                                 ("k", psK, tk, sk2, 4 + h)):
                    if tsem.n > 0:
                        P.wait("tensor", ssem, tsem.n)
                    if nm == "q" and tv.n > 0:
                        P.wait("tensor", vv, tv.n)
                    wv_ = w0sb[nm].rearrange("p (c f) -> p c f", c=NCH)

                    def mmrun(e, wv_=wv_, xv=xv, pq=pq, h=h):
                        ins = None
                        for c in range(NCH):
                            ins = e.matmul(
                                out=pq[:],
                                lhsT=wv_[:, c, h * 128:(h + 1) * 128],
                                rhs=xv[:, c, :],
                                start=(c == 0), stop=(c == NCH - 1))
                        return ins
                    P.do("tensor", mmrun, tsem)
                    P.wait("scalar", tsem, tsem.n)
                    if nm == "q":
                        P.do("scalar", lambda e, h=h, bcol=bcol: e.activation(
                            out=qb[:, h * 512:(h + 1) * 512], in_=psQ[:],
                            func=ACT.Identity, bias=b0scl[:, bcol:bcol + 1],
                            scale=SCALE0), ssem)
                    else:
                        P.do("scalar", lambda e, h=h, tt=tt, bcol=bcol:
                             e.activation(
                                 out=kb[:, h * TB + tt * 512:
                                        h * TB + (tt + 1) * 512],
                                 in_=psK[:], func=ACT.Identity,
                                 bias=b0scl[:, bcol:bcol + 1],
                                 scale=SCALE0), ssem)
            wvv = w0sb["v"]
            for sub in range(4):
                P.wait("tensor", sq, tq.n)
                if tv.n > 0:
                    P.wait("tensor", vv, tv.n)

                def vmm(e, xv=xv, sub=sub):
                    ins = None
                    for c in range(NCH):
                        ins = e.matmul(
                            out=psV[:],
                            lhsT=xv[:, c, sub * 128:(sub + 1) * 128],
                            rhs=wvv[:, c * 512:(c + 1) * 512],
                            start=(c == 0), stop=(c == NCH - 1))
                    return ins
                P.do("tensor", vmm, tv)
                P.wait("vector", tv, tv.n)
                kt_tile = tt * 4 + sub
                P.do("vector", lambda e, kt_tile=kt_tile: e.tensor_add(
                    out=vb[:, kt_tile * 512:(kt_tile + 1) * 512],
                    in0=psV[:], in1=bv0sb[:]), vv)
            for i in (2 * tt, 2 * tt + 1):
                if i < NCH:
                    fk_fold_step(i)

            nkt = 4 * tt + 4
            for h in range(NH):
                if vo.n > 0:
                    P.wait("tensor", vo, vo.n)
                kt0 = kt_glob

                def emit_score(kt, h=h, tt=tt, kt0=kt0):
                    g = kt0 + kt
                    psSp = psS if g % 2 == 0 else psS2
                    pbs = pb[:, (g % 2) * 512:(g % 2 + 1) * 512]
                    if g > 1:
                        P.wait("tensor", se, g - 1)
                    P.do("tensor", lambda e, h=h, kt=kt, psSp=psSp: e.matmul(
                        out=psSp[:],
                        lhsT=kb[:, h * TB + kt * 128:h * TB + (kt + 1) * 128],
                        rhs=qb[:, h * 512:(h + 1) * 512],
                        start=True, stop=True), tsc)
                    P.wait("scalar", tsc, tsc.n)
                    if g > 1:
                        P.wait("scalar", ta, g - 1)
                    P.do("scalar", lambda e, psSp=psSp, pbs=pbs: e.activation(
                        out=pbs, in_=psSp[:], func=ACT.Exp), se)
                    if kt >= 4 * tt:
                        j = kt - 4 * tt
                        P.wait("vector", se, se.n)
                        P.do("vector", lambda e, j=j, pbs=pbs: e.tensor_mul(
                            out=pbs, in0=pbs,
                            in1=m0[:, j * 512:(j + 1) * 512]), vm)

                emit_score(0)
                for kt in range(nkt):
                    g = kt0 + kt
                    pbs = pb[:, (g % 2) * 512:(g % 2 + 1) * 512]
                    diag = kt >= 4 * tt
                    if kt + 1 < nkt:
                        emit_score(kt + 1)
                    if diag:
                        P.wait("tensor", vm, vm.n)
                    else:
                        P.wait("tensor", se, g + 1)
                    P.do("tensor", lambda e, kt=kt, h=h, nkt=nkt, pbs=pbs:
                         e.matmul(
                             out=psAV[:],
                             lhsT=vb[:, kt * 512 + h * 128:
                                     kt * 512 + (h + 1) * 128],
                             rhs=pbs, start=(kt == 0), stop=(kt == nkt - 1)))
                    P.do("tensor", lambda e, kt=kt, nkt=nkt, pbs=pbs: e.matmul(
                        out=psDen[:], lhsT=onesc[:], rhs=pbs,
                        start=(kt == 0), stop=(kt == nkt - 1)), ta)
                    kt_glob += 1
                P.wait("vector", ta, ta.n)
                P.do("vector", lambda e: e.tensor_copy(out=rows[:, 0:512],
                                                       in_=psDen[:]))
                P.do("vector", lambda e: e.reciprocal(out=rows[:, 512:1024],
                                                      in_=rows[:, 0:512]))
                P.do("vector", lambda e: e.tensor_copy(
                    out=rowb[:, 0:512], in_=rows[:, 512:1024]))
                P.do("vector", lambda e: e.tensor_copy(
                    out=rows[:, 1024:1536], in_=rowb[:, 0:512]))
                P.do("vector", lambda e: e.tensor_sub(
                    out=rows[:, 1024:1536], in0=rows[:, 512:1024],
                    in1=rows[:, 1024:1536]))
                P.do("vector", lambda e: e.tensor_copy(
                    out=rowb[:, 512:1024], in_=rows[:, 1024:1536]), vr)
                P.wait("tensor", vr, vr.n)
                P.do("tensor", lambda e: e.matmul(
                    out=psBC[:], lhsT=onesr[:], rhs=rowb[:, 0:512],
                    start=True, stop=False))
                P.do("tensor", lambda e: e.matmul(
                    out=psBC[:], lhsT=onesr[:], rhs=rowb[:, 512:1024],
                    start=False, stop=True), tb)
                P.wait("vector", tb, tb.n)
                if o_idx >= 2:
                    P.wait("vector", ow[o_idx % 2], ow[o_idx % 2].n)
                P.do("vector", lambda e: e.tensor_copy(out=rdsb[:],
                                                       in_=psBC[:]))
                P.do("vector", lambda e, oi=o_idx: e.tensor_mul(
                    out=osb[:, (oi % 2) * 512:(oi % 2 + 1) * 512],
                    in0=psAV[:], in1=rdsb[:]), vo)
                P.wait("sync", vo, vo.n)
                P.do("sync", lambda e, h=h, tt=tt, oi=o_idx: e.dma_start(
                    out=cin0[h * 128:(h + 1) * 128,
                             tt * 512:(tt + 1) * 512],
                    in_=osb[:, (oi % 2) * 512:(oi % 2 + 1) * 512]),
                    ow[o_idx % 2], 16)
                o_idx += 1
            fk_load_step(2 * tt + 2)
            fk_load_step(2 * tt + 3)

        P.wait("vector", fkv, NCH)
        for j in range(NCH):
            P.do("vector", lambda e, j=j: e.tensor_copy(
                out=fkscb[:, 0:128], in_=fkmax[:, j * 128:(j + 1) * 128]))
            P.do("vector", lambda e: e.tensor_copy(out=fksc[:, 0:128],
                                                   in_=fkscb[:, 0:128]))
            P.do("vector", lambda e, j=j: e.tensor_sub(
                out=fksc[:, 0:128], in0=fkmax[:, j * 128:(j + 1) * 128],
                in1=fksc[:, 0:128]))
            P.do("vector", lambda e: e.tensor_copy(out=fkscb[:, 128:256],
                                                   in_=fksc[:, 0:128]), vx)
            P.wait("tensor", vx, vx.n)
            P.do("tensor", lambda e: e.transpose(
                out=psT[:], in_=fkscb[:, 0:128], identity=eyeb[:]), tx)
            P.wait("vector", tx, tx.n)
            P.do("vector", lambda e: e.tensor_copy(out=fksc[:, 0:128],
                                                   in_=psT[:]), vx)
            P.wait("tensor", vx, vx.n)
            P.do("tensor", lambda e: e.transpose(
                out=psT[:], in_=fkscb[:, 128:256], identity=eyeb[:]), tx)
            P.wait("vector", tx, tx.n)
            P.do("vector", lambda e: e.tensor_copy(out=fksc[:, 128:256],
                                                   in_=psT[:]))
            P.do("vector", lambda e: e.tensor_add(
                out=fksc[:, 128:256], in0=fksc[:, 128:256],
                in1=fksc[:, 0:128]))
            P.do("vector", lambda e, j=j: e.reduce_max(
                out=colsA[:, j:j + 1], in_=fksc[:, 128:256], axis=AX.X), vx)
            P.wait("tensor", vx, vx.n)
        P.do("vector", lambda e: e.tensor_scalar_max(
            out=colsA[:, 0:NCH], in0=colsA[:, 0:NCH], scalar1=1e-30))
        P.do("vector", lambda e: e.tensor_scalar_mul(
            out=skcol[:], in0=colsA[:, 0:NCH], scalar1=1.0 / 127.0))
        P.do("vector", lambda e: e.reciprocal(out=colsA[:, 16:32],
                                              in_=colsA[:, 0:NCH]))
        P.do("vector", lambda e: e.tensor_scalar_mul(
            out=colsA[:, 16:32], in0=colsA[:, 16:32], scalar1=127.0))
        P.do("vector", lambda e: e.tensor_copy(out=colsB[:, 0:16],
                                               in_=colsA[:, 16:32]))
        P.do("vector", lambda e: e.tensor_sub(out=colsA[:, 0:16],
                                              in0=colsA[:, 16:32],
                                              in1=colsB[:, 0:16]))
        P.do("vector", lambda e: e.tensor_copy(out=colsB[:, 16:32],
                                               in_=colsA[:, 0:16]), vx)
        P.wait("tensor", vx, vx.n)
        for j in range(NCH):
            P.do("tensor", lambda e, j=j: e.matmul(
                out=psDen[:, (j % 4) * 128:(j % 4 + 1) * 128],
                lhsT=colsB[:, j:j + 1], rhs=eyeb[:],
                start=True, stop=False))
            P.do("tensor", lambda e, j=j: e.matmul(
                out=psDen[:, (j % 4) * 128:(j % 4 + 1) * 128],
                lhsT=colsB[:, 16 + j:17 + j], rhs=eyeb[:],
                start=False, stop=True), tx)
            if j % 4 == 3:
                P.wait("vector", tx, tx.n)
                P.do("vector", lambda e, j=j: e.tensor_copy(
                    out=rkrow[:, (j - 3) * 128:(j + 1) * 128],
                    in_=psDen[:]), vx)
                P.wait("tensor", vx, vx.n)

        P.do("vector", lambda e: e.reduce_max(
            out=colsA[:, 0:1], in_=fb0sb[:], axis=AX.X,
            apply_absolute_value=True))
        P.do("vector", lambda e: e.tensor_copy(out=colsB[:, 0:1],
                                               in_=colsA[:, 0:1]))
        P.do("vector", lambda e: e.tensor_sub(out=colsA[:, 1:2],
                                              in0=colsA[:, 0:1],
                                              in1=colsB[:, 0:1]))
        P.do("vector", lambda e: e.tensor_copy(out=colsB[:, 1:2],
                                               in_=colsA[:, 1:2]), vx)
        P.wait("tensor", vx, vx.n)
        P.do("tensor", lambda e: e.matmul(out=psDen[:, 0:128],
                                          lhsT=colsB[:, 0:1], rhs=eyeb[:],
                                          start=True, stop=False))
        P.do("tensor", lambda e: e.matmul(out=psDen[:, 0:128],
                                          lhsT=colsB[:, 1:2], rhs=eyeb[:],
                                          start=False, stop=True), tx)
        P.wait("vector", tx, tx.n)
        P.do("vector", lambda e: e.tensor_copy(out=rows[:, 0:128],
                                               in_=psDen[:, 0:128]))
        P.do("vector", lambda e: e.reduce_max(out=rows[:, 128:129],
                                              in_=rows[:, 0:128], axis=AX.X))
        P.do("vector", lambda e: e.tensor_scalar_max(
            out=rows[:, 128:129], in0=rows[:, 128:129], scalar1=1e-30))
        P.do("vector", lambda e: e.reciprocal(out=rows[:, 129:130],
                                              in_=rows[:, 128:129]))
        P.do("vector", lambda e: e.tensor_scalar_mul(
            out=rows[:, 129:130], in0=rows[:, 129:130], scalar1=127.0))
        P.do("vector", lambda e: e.tensor_scalar_mul(
            out=rows[:, 128:129], in0=rows[:, 128:129], scalar1=1.0 / 127.0))
        P.do("vector", lambda e: e.tensor_copy(out=rowb[:, 0:2],
                                               in_=rows[:, 128:130]), vx)
        P.wait("tensor", vx, vx.n)
        P.do("tensor", lambda e: e.matmul(out=psBC[:, 0:2], lhsT=onesr[:],
                                          rhs=rowb[:, 0:2], start=True,
                                          stop=True), tx)
        P.wait("vector", tx, tx.n)
        P.do("vector", lambda e: e.tensor_copy(out=colsA[:, 0:2],
                                               in_=psBC[:, 0:2]))
        P.do("vector", lambda e: e.tensor_scalar(
            out=bqq[:], in0=fb0sb[:], scalar1=colsA[:, 1:2], scalar2=None,
            op0=ALU.mult))
        P.do("vector", lambda e: e.tensor_scalar(
            out=bqq[:], in0=bqq[:], scalar1=RND_C, scalar2=RND_C,
            op0=ALU.add, op1=ALU.subtract))
        P.do("vector", lambda e: e.tensor_scalar(
            out=bqq[:], in0=bqq[:], scalar1=colsA[:, 0:1], scalar2=None,
            op0=ALU.mult), vx)

        P.wait("gpsimd", ow[0], ow[0].n)
        P.wait("gpsimd", ow[1], ow[1].n)
        P.do("gpsimd", lambda e: e.collective_compute(
            "AllGather", ALU.bypass, ins=[cin0[:]], outs=[cout0[:]],
            replica_groups=groups), cc)

        s1.close()
        phase_barrier()

        s2 = ExitStack()
        strm = sbuf(s2, "strm", [128, NCH * SL])
        lnb = sbuf(s2, "lnb", [128, 2 * SL], BF16)
        psA = psum(s2, "psA", [128, 512])
        psB = psum(s2, "psB", [128, 512])
        psR = [psum(s2, f"psR{i}", [1, 512]) for i in range(4)]
        psF = psum(s2, "psF", [128, 1024], BF16)
        s2a = ExitStack()
        otok = sbuf(s2a, "otok", [128, NCH * SL], BF16)
        wosb = sbuf(s2a, "wosb", [128, 2 * NCH * 128], BF16)

        P.do("sync", lambda e: e.dma_start(
            out=strm.rearrange("p (c t) -> p c t", c=NCH),
            in_=xown[:]), sld, 16)

        P.wait("gpsimd", cc, cc.n)
        coutv = cout0.rearrange("(c p) (q t) -> q p c t", p=128, q=4)

        def ogather(e):
            with nc.gpsimd.register() as r:
                e.reg_load(r, pidxsb[0:1, 0:1])
                idx = e.snap(r, min_val=0, max_val=3)
                ins = e.dma_start(
                    out=otok.rearrange("p (c t) -> p c t", c=NCH),
                    in_=coutv[idx])
            return ins
        P.do("gpsimd", ogather, go, 16)
        tx_base2 = tx.n
        for c in range(NCH):
            if c >= 2:
                P.wait("gpsimd", tx, tx_base2 + 2 * (c - 1))
            P.do("gpsimd", lambda e, c=c: e.dma_start(
                out=wosb[:, (c % 2) * 2048:(c % 2 + 1) * 2048]
                .rearrange("p (c2 f) -> p c2 f", c2=NCH),
                in_=wo0t[c]), wo_s[c % 2], 16)

        otv = otok.rearrange("p (c t) -> p c t", c=NCH)
        vx_base2 = vx.n
        P.wait("vector", sld, 16)
        P.wait("tensor", go, 16)
        for c in range(NCH):
            P.wait("tensor", wo_s[c % 2], (c // 2 + 1) * 16)
            for th in range(2):
                pq = psA if th == 0 else psB
                if c >= 1:
                    P.wait("tensor", vx, vx_base2 + 2 * (c - 1) + th + 1)

                def womm(e, c=c, th=th, pq=pq):
                    w = wosb[:, (c % 2) * 2048:(c % 2 + 1) * 2048].rearrange(
                        "p (c2 f) -> p c2 f", c2=NCH)
                    ins = None
                    for ech in range(NCH):
                        ins = e.matmul(
                            out=pq[:], lhsT=w[:, ech, :],
                            rhs=otv[:, ech, th * 512:(th + 1) * 512],
                            start=(ech == 0), stop=(ech == NCH - 1))
                    return ins
                P.do("tensor", womm, tx)
                P.wait("vector", tx, tx.n)
                P.do("vector", lambda e, c=c, th=th, pq=pq:
                     e.scalar_tensor_tensor(
                         out=strm[:, c * SL + th * 512:
                                  c * SL + (th + 1) * 512],
                         in0=pq[:], scalar=bo0sb[:, c:c + 1],
                         in1=strm[:, c * SL + th * 512:
                                  c * SL + (th + 1) * 512],
                         op0=ALU.add, op1=ALU.add), vx)

        s2a.close()

        def emit_layernorm(psA, psB, psR, lnb, strm_t):
            stat_tx = tx.n
            P.wait("scalar", vx, vx.n)
            for c in range(NCH):
                if c >= 1:
                    P.wait("scalar", tx, stat_tx + c)
                P.do("scalar", lambda e, c=c: e.copy(
                    out=lnb[:, 0:SL], in_=strm_t[:, c * SL:(c + 1) * SL]))
                P.do("scalar", lambda e, c=c: e.activation(
                    out=lnb[:, SL:2 * SL], in_=strm_t[:, c * SL:(c + 1) * SL],
                    func=ACT.Square), sx2)
                P.wait("tensor", sx2, sx2.n)
                for th in range(2):
                    P.do("tensor", lambda e, c=c, th=th: e.matmul(
                        out=psR[th][:], lhsT=onesc[:],
                        rhs=lnb[:, th * 512:(th + 1) * 512],
                        start=(c == 0), stop=(c == NCH - 1)))
                    P.do("tensor", lambda e, c=c, th=th: e.matmul(
                        out=psR[2 + th][:], lhsT=onesc[:],
                        rhs=lnb[:, SL + th * 512:SL + (th + 1) * 512],
                        start=(c == 0), stop=(c == NCH - 1)),
                        tx if th == 1 else None)
            P.wait("vector", tx, tx.n)
            for th in range(2):
                P.do("vector", lambda e, th=th: e.tensor_scalar_mul(
                    out=rows[:, th * 512:(th + 1) * 512], in0=psR[th][:],
                    scalar1=1.0 / D))
                P.do("vector", lambda e, th=th: e.tensor_scalar_mul(
                    out=rows[:, 1024 + th * 512:1024 + (th + 1) * 512],
                    in0=psR[2 + th][:], scalar1=1.0 / D))
            P.do("vector", lambda e: e.tensor_mul(
                out=rows[:, 2048:3072], in0=rows[:, 0:1024],
                in1=rows[:, 0:1024]))
            P.do("vector", lambda e: e.tensor_sub(
                out=rows[:, 1024:2048], in0=rows[:, 1024:2048],
                in1=rows[:, 2048:3072]), vx)
            P.do("vector", lambda e: e.tensor_scalar_add(
                out=rows[:, 1024:2048], in0=rows[:, 1024:2048],
                scalar1=EPS_LN), vx)
            P.wait("scalar", vx, vx.n)
            P.do("scalar", lambda e: e.activation(
                out=rows[:, 2048:3072], in_=rows[:, 1024:2048],
                func=ACT.Sqrt), sx2)
            P.wait("vector", sx2, sx2.n)
            P.do("vector", lambda e: e.reciprocal(
                out=rows[:, 3072:4096], in_=rows[:, 2048:3072]))
            P.do("vector", lambda e: e.tensor_mul(
                out=rows[:, 0:1024], in0=rows[:, 0:1024],
                in1=rows[:, 3072:4096]), vx)
            for th in range(2):
                P.wait("tensor", vx, vx.n)
                P.do("tensor", lambda e, th=th: e.matmul(
                    out=psA[:], lhsT=onesrF[:],
                    rhs=rows[:, 3072 + th * 512:3072 + (th + 1) * 512],
                    start=True, stop=True))
                P.do("tensor", lambda e, th=th: e.matmul(
                    out=psB[:], lhsT=onesrF[:],
                    rhs=rows[:, th * 512:(th + 1) * 512],
                    start=True, stop=True), tx)
                P.wait("vector", tx, tx.n)
                for c in range(NCH):
                    sl_ = slice(c * SL + th * 512, c * SL + (th + 1) * 512)
                    P.do("vector", lambda e, s=sl_: e.tensor_mul(
                        out=strm_t[:, s], in0=strm_t[:, s], in1=psA[:]))
                    P.do("vector", lambda e, s=sl_: e.tensor_sub(
                        out=strm_t[:, s], in0=strm_t[:, s], in1=psB[:]),
                        vx if c == NCH - 1 else None)

        emit_layernorm(psA, psB, psR, lnb, strm)

        if stop_after >= 3:
            phase_barrier()
            s25 = ExitStack()
            fkb3 = sbuf(s25, "fkb3", [128, D])
            qs3 = sbuf(s25, "qs3", [128, D])
            rkt3 = sbuf(s25, "rkt3", [128, D], BF16)
            rkbsb = sbuf(s25, "rkbsb", [128, D])
            P.wait("tensor", vx, vx.n)
            vx_b25 = vx.n
            for j4 in range(4):
                pq25 = psA if j4 % 2 == 0 else psB
                if j4 >= 2:
                    P.wait("tensor", vx, vx_b25 + j4 - 1)
                P.do("tensor", lambda e, j4=j4, pq25=pq25: e.matmul(
                    out=pq25[:], lhsT=onesrF[:],
                    rhs=rkrow[:, j4 * 512:(j4 + 1) * 512],
                    start=True, stop=True), tx)
                P.wait("vector", tx, tx.n)
                P.do("vector", lambda e, j4=j4, pq25=pq25: e.tensor_copy(
                    out=rkbsb[:, j4 * 512:(j4 + 1) * 512], in_=pq25[:]), vx)
            P.wait("tensor", vx, vx.n)
            fkl_base3 = fkl.n
            fkv_base3 = fkv.n
            for dch in range(NCH):
                if dch >= 1:
                    P.wait("sync", fkv, fkv_base3 + dch)
                P.do("sync", lambda e, dch=dch: e.dma_start(
                    out=fkb3[:], in_=fk0[dch]), fkl, 16)
                P.wait("vector", fkl, fkl.n)
                if dch >= 1:
                    P.wait("vector", rkw, dch * 16)
                P.do("vector", lambda e: e.tensor_mul(
                    out=qs3[:], in0=fkb3[:], in1=rkbsb[:]))
                P.do("vector", lambda e: e.tensor_scalar(
                    out=rkt3[:], in0=qs3[:], scalar1=RND_C, scalar2=RND_C,
                    op0=ALU.add, op1=ALU.subtract), fkv)
                P.wait("sync", fkv, fkv.n)
                P.do("sync", lambda e, dch=dch: e.dma_start(
                    out=rkq[:, :, dch, :].transpose([1, 0, 2]),
                    in_=rkt3.rearrange("p (f q) -> p f q", f=NCH)), rkw, 16)
            s25.close()
            phase_barrier()

            s3 = ExitStack()
            y3 = sbuf(s3, "y3", [128, NCH * 512])
            rxq = sbuf(s3, "rxq", [128, NCH * 512], BF16)
            rkf = sbuf(s3, "rkf", [128, 2 * NCH * 128], BF16)
            sxsb = sbuf(s3, "sxsb", [128, 512])
            rxsb = sbuf(s3, "rxsb", [128, 512])
            rysb = sbuf(s3, "rysb", [128, 512])
            sysb = sbuf(s3, "sysb", [128, 512])
            ym3 = sbuf(s3, "ym3", [128, 512])
            t3 = sbuf(s3, "t3", [128, 512])
            t3b = sbuf(s3, "t3b", [128, 1024], BF16)

            def part_fold_tokens(src_getter, rcol, scol, rmul, smul):
                P.do("vector", lambda e: e.tensor_copy(
                    out=t3b[:, 0:512], in_=src_getter()[:]))
                P.do("vector", lambda e: e.tensor_sub(
                    out=t3[:], in0=src_getter()[:], in1=t3b[:, 0:512]))
                P.do("vector", lambda e: e.tensor_copy(
                    out=t3b[:, 512:1024], in_=t3[:]), vx)
                P.wait("tensor", vx, vx.n)
                for j2 in range(4):
                    P.do("tensor", lambda e, j2=j2: e.transpose(
                        out=psF[:, j2 * 128:(j2 + 1) * 128],
                        in_=t3b[:, j2 * 128:(j2 + 1) * 128],
                        identity=eyeb[:]))
                    P.do("tensor", lambda e, j2=j2: e.transpose(
                        out=psF[:, 512 + j2 * 128:512 + (j2 + 1) * 128],
                        in_=t3b[:, 512 + j2 * 128:512 + (j2 + 1) * 128],
                        identity=eyeb[:]), tx if j2 == 3 else None)
                P.wait("vector", tx, tx.n)
                P.do("vector", lambda e: e.tensor_copy(
                    out=t3[:], in_=psF[:, 0:512]))
                P.do("vector", lambda e: e.tensor_add(
                    out=t3[:], in0=t3[:], in1=psF[:, 512:1024]))
                for j2 in range(4):
                    P.do("vector", lambda e, j2=j2: e.reduce_max(
                        out=colsA[:, j2:j2 + 1],
                        in_=t3[:, j2 * 128:(j2 + 1) * 128], axis=AX.X))
                P.do("vector", lambda e: e.tensor_scalar_max(
                    out=colsA[:, 0:4], in0=colsA[:, 0:4], scalar1=1e-30))
                P.do("vector", lambda e, smul=smul: e.tensor_scalar_mul(
                    out=colsA[:, 8:12], in0=colsA[:, 0:4], scalar1=smul))
                P.do("vector", lambda e: e.reciprocal(
                    out=colsA[:, 4:8], in_=colsA[:, 0:4]))
                P.do("vector", lambda e, rmul=rmul: e.tensor_scalar_mul(
                    out=colsA[:, 4:8], in0=colsA[:, 4:8], scalar1=rmul))
                P.do("vector", lambda e: e.tensor_copy(
                    out=colsB[:, 0:8], in_=colsA[:, 4:12]))
                P.do("vector", lambda e: e.tensor_sub(
                    out=colsA[:, 12:20], in0=colsA[:, 4:12],
                    in1=colsB[:, 0:8]))
                P.do("vector", lambda e: e.tensor_copy(
                    out=colsB[:, 8:16], in_=colsA[:, 12:20]), vx)
                P.wait("tensor", vx, vx.n)
                for j in range(8):
                    pr = psR[0] if j < 4 else psR[1]
                    off = (j % 4) * 128
                    P.do("tensor", lambda e, j=j, pr=pr, off=off: e.matmul(
                        out=pr[:, off:off + 128],
                        lhsT=colsB[:, j:j + 1], rhs=eyeb[:],
                        start=True, stop=False))
                    P.do("tensor", lambda e, j=j, pr=pr, off=off: e.matmul(
                        out=pr[:, off:off + 128],
                        lhsT=colsB[:, 8 + j:9 + j], rhs=eyeb[:],
                        start=False, stop=True), tx if j == 7 else None)
                P.wait("vector", tx, tx.n)
                P.do("vector", lambda e: e.tensor_copy(
                    out=rows[:, 0:512], in_=psR[0][:]))
                P.do("vector", lambda e: e.tensor_copy(
                    out=rows[:, 512:1024], in_=psR[1][:]), vx)
                P.wait("tensor", vx, vx.n)
                P.do("tensor", lambda e: e.matmul(
                    out=psA[:], lhsT=onesrF[:], rhs=rows[:, 0:512],
                    start=True, stop=True))
                P.do("tensor", lambda e: e.matmul(
                    out=psB[:], lhsT=onesrF[:], rhs=rows[:, 512:1024],
                    start=True, stop=True), tx)
                P.wait("vector", tx, tx.n)
                P.do("vector", lambda e, rcol=rcol: e.tensor_copy(
                    out=rcol[:], in_=psA[:]))
                P.do("vector", lambda e, scol=scol: e.tensor_copy(
                    out=scol[:], in_=psB[:]), vx)
                P.wait("tensor", vx, vx.n)

            for th in range(2):
                tsl = lambda c, th=th: slice(c * SL + th * 512,
                                             c * SL + (th + 1) * 512)
                sl0 = tsl(0)
                P.do("vector", lambda e, sl0=sl0: e.scalar_tensor_tensor(
                    out=ym3[:], in0=strm[:, sl0], scalar=-1.0,
                    in1=strm[:, sl0], op0=ALU.mult, op1=ALU.max))
                for c in range(1, NCH):
                    slc = tsl(c)
                    P.do("vector", lambda e, slc=slc:
                         e.scalar_tensor_tensor(
                             out=t3[:], in0=strm[:, slc], scalar=-1.0,
                             in1=strm[:, slc], op0=ALU.mult, op1=ALU.max))
                    P.do("vector", lambda e: e.tensor_max(
                        out=ym3[:], in0=ym3[:], in1=t3[:]),
                        vx if c == NCH - 1 else None)
                if debug3 and th == 0:
                    P.wait("sync", vx, vx.n)
                    P.do("sync", lambda e: e.dma_start(
                        out=dbgD[:, 0:512], in_=ym3[:]), st, 16)
                    P.wait("vector", st, st.n)
                part_fold_tokens(lambda: ym3, rxsb, sxsb, 127.0, 1.0 / 127.0)
                if debug3 and th == 0:
                    P.wait("sync", vx, vx.n)
                    P.do("sync", lambda e: e.dma_start(
                        out=dbgD[:, 512:544], in_=colsA[:]), st, 16)
                    P.wait("vector", st, st.n)
                for c in range(NCH):
                    slc = tsl(c)
                    P.do("vector", lambda e, slc=slc: e.tensor_mul(
                        out=t3[:], in0=strm[:, slc], in1=rxsb[:]))
                    P.do("vector", lambda e, c=c: e.tensor_scalar(
                        out=rxq[:, c * 512:(c + 1) * 512], in0=t3[:],
                        scalar1=RND_C, scalar2=RND_C,
                        op0=ALU.add, op1=ALU.subtract),
                        vx if c == NCH - 1 else None)
                rxv = rxq.rearrange("p (c t) -> p c t", c=NCH)
                vx_b3 = vx.n
                rk_base = [rk_s[0].n, rk_s[1].n]
                P.wait("sync", rkw, rkw.n)
                for fch in range(NCH):
                    if ty.n + fch >= 2:
                        P.wait("sync", ty, ty.n + fch - 1)
                    P.do("sync", lambda e, fch=fch: e.dma_start(
                        out=rkf[:, (fch % 2) * 2048:(fch % 2 + 1) * 2048]
                        .rearrange("p (d q) -> p d q", d=NCH),
                        in_=rkq[fch]), rk_s[fch % 2], 16)
                P.wait("tensor", vx, vx.n)
                for fch in range(NCH):
                    pq = psA if fch % 2 == 0 else psB
                    P.wait("tensor", rk_s[fch % 2],
                           rk_base[fch % 2] + (fch // 2 + 1) * 16)
                    if fch >= 2:
                        P.wait("tensor", vx, vx_b3 + fch - 1)

                    def ymm(e, fch=fch, pq=pq):
                        w = rkf[:, (fch % 2) * 2048:(fch % 2 + 1) * 2048] \
                            .rearrange("p (d q) -> p d q", d=NCH)
                        ins = None
                        for dch in range(NCH):
                            ins = e.matmul(out=pq[:], lhsT=w[:, dch, :],
                                           rhs=rxv[:, dch, :],
                                           start=(dch == 0),
                                           stop=(dch == NCH - 1))
                        return ins
                    P.do("tensor", ymm, ty)
                    P.wait("vector", ty, ty.n)
                    P.do("vector", lambda e, pq=pq: e.tensor_mul(
                        out=t3[:], in0=pq[:], in1=sxsb[:]), vx)
                    P.wait("scalar", vx, vx.n)
                    P.do("scalar", lambda e, fch=fch: e.activation(
                        out=y3[:, fch * 512:(fch + 1) * 512], in_=t3[:],
                        func=ACT.Relu, bias=bqq[:, fch:fch + 1],
                        scale=skcol[:, fch:fch + 1]), sx2)
                    P.wait("vector", sx2, sx2.n)
                    if fch == 0:
                        P.do("vector", lambda e: e.tensor_copy(
                            out=ym3[:], in_=y3[:, 0:512]))
                    else:
                        P.do("vector", lambda e, fch=fch: e.tensor_max(
                            out=ym3[:], in0=ym3[:],
                            in1=y3[:, fch * 512:(fch + 1) * 512]))
                if debug3 and th == 0:
                    P.wait("sync", vx, vx.n)
                    P.wait("sync", sx2, sx2.n)
                    P.do("sync", lambda e: e.dma_start(out=dbgA[:], in_=y3[:]),
                         st, 16)
                    P.do("sync", lambda e: e.dma_start(out=dbgB[:], in_=rxq[:]),
                         st, 16)
                    P.do("sync", lambda e: e.dma_start(
                        out=dbgC[:, 0:512], in_=rxsb[:]), st, 16)
                    P.do("sync", lambda e: e.dma_start(
                        out=dbgC[:, 512:1024], in_=sxsb[:]), st, 16)
                    P.wait("vector", st, st.n)
                    P.wait("scalar", st, st.n)
                part_fold_tokens(lambda: ym3, rysb, sysb, 127.0, 1.0 / 127.0)
                if debug3 and th == 0:
                    P.wait("sync", vx, vx.n)
                    P.do("sync", lambda e: e.dma_start(
                        out=dbgC[:, 1024:1536], in_=rysb[:]), st, 16)
                    P.do("sync", lambda e: e.dma_start(
                        out=dbgC[:, 1536:2048], in_=sysb[:]), st, 16)
                    P.wait("vector", st, st.n)
                for c in range(NCH):
                    slc = tsl(c)
                    P.do("vector", lambda e, c=c: e.tensor_mul(
                        out=t3[:], in0=y3[:, c * 512:(c + 1) * 512],
                        in1=rysb[:]))
                    P.do("vector", lambda e: e.tensor_scalar(
                        out=t3[:], in0=t3[:], scalar1=RND_C, scalar2=RND_C,
                        op0=ALU.add, op1=ALU.subtract))
                    P.do("vector", lambda e: e.tensor_mul(
                        out=t3[:], in0=t3[:], in1=sysb[:]))
                    P.do("vector", lambda e, slc=slc: e.tensor_add(
                        out=strm[:, slc], in0=strm[:, slc],
                        in1=t3[:]), vx)
            s3.close()
            emit_layernorm(psA, psB, psR, lnb, strm)
            phase_barrier()


        def emit_output(strm_t):
            P.wait("sync", vx, vx.n)
            for c in range(NCH):
                P.do("sync", lambda e, c=c: e.dma_start(
                    out=y[c * 128:(c + 1) * 128, :],
                    in_=strm_t[:, c * SL:(c + 1) * SL]), st, 16)
            P.wait("sync", st, st.n)

        if stop_after <= 3:
            emit_output(strm)
            s2.close()
            P.emit(block)
            return nc

        P.wait("sync", vx, vx.n)
        P.do("sync", lambda e: e.dma_start(
            out=strm_d[:],
            in_=strm.rearrange("p (c t) -> p c t", c=NCH)), S["sdw"], 16)
        phase_barrier()
        s2.close()
        s123.close()

        s4 = ExitStack()
        vgrp4 = sbuf(s4, "vgrp4", [128, 4 * 8 * 512], BF16)

        s4A = ExitStack()
        xb4 = sbuf(s4A, "xb4", [128, NCH * SL], BF16)
        ropeCb = sbuf(s4A, "ropeCb", [128, SL])
        ropeSb = sbuf(s4A, "ropeSb", [128, SL])
        bv1sb = sbuf(s4A, "bv1sb", [128, D])
        qb4 = sbuf(s4A, "qb4", [128, 4 * SL], BF16)
        kb4 = sbuf(s4A, "kb4", [128, 4 * SL], BF16)
        gb4 = sbuf(s4A, "gb4", [128, 4 * SL], BF16)
        ktm4 = sbuf(s4A, "ktm4", [128, 4 * SL], BF16)
        A4 = sbuf(s4A, "A4", [128, SL])
        As4 = sbuf(s4A, "As4", [128, SL])
        t4a = sbuf(s4A, "t4a", [128, SL])
        wsl = sbuf(s4A, "wsl", [128, 2 * D], BF16)
        wvg = sbuf(s4A, "wvg", [128, NCH * 512], BF16)
        sloc = sbuf(s4A, "sloc", [128, NCH * 132])
        psP = psum(s4A, "psP", [128, SL])
        psV4 = psum(s4A, "psV4", [128, 512])
        psT4 = psum(s4A, "psT4", [128, 128], BF16)
        psSZ = psum(s4A, "psSZ", [128, 129])

        P.do("gpsimd", lambda e: e.dma_start(
            out=xb4.rearrange("p (c t) -> p c t", c=NCH),
            in_=strm_d[:]), gd, 16)
        P.do("gpsimd", lambda e: e.dma_start(out=bv1sb[:], in_=bv1b[:]),
             gd, 16)
        P.do("sync", lambda e: e.dma_start(out=ropeCb[:], in_=ropeC[:]),
             S["s40"], 16)
        P.do("sync", lambda e: e.dma_start(out=ropeSb[:], in_=ropeS[:]),
             S["s40"], 16)
        for eng in ("tensor", "vector", "scalar"):
            P.wait(eng, gd)
            P.wait(eng, S["s40"])

        P.do("vector", lambda e: e.memset(sloc[:], 0.0), S["v4"])
        xv4 = xb4.rearrange("p (c t) -> p c t", c=NCH)
        wslv = [wsl[:, i * D:(i + 1) * D]
                .rearrange("p (c f) -> p c f", c=NCH) for i in range(2)]
        wsem = [S["g40"], S["g41"]]
        wbase = [S["g40"].n, S["g41"].n]
        wsrc = {"q": (wq1t, 0), "k": (wk1t, 16), "g": (wg1t, 32)}
        shufmask = []
        for i in range(16):
            shufmask += [2 * i + 1, 2 * i]

        t4, t4v, t4t, v4, v4b, s4e = (S[n] for n in
                                      ["t4", "t4v", "t4t", "v4", "v4b", "s4e"])
        tv_b4 = tv.n
        vv_b4 = vv.n
        nw = 0
        for g4 in range(4):
            heads = [4 * g4 + i for i in range(4)]
            if g4 > 0:
                P.wait("gpsimd", tv, tv_b4 + 8 * g4)
                P.wait("gpsimd", S["g4v"], 16 * g4)
            P.do("gpsimd", lambda e, g4=g4: e.dma_start(
                out=wvg.rearrange("p (c f) -> p c f", c=NCH),
                in_=wv1t[g4]), S["g4v"], 16)
            for pi in ("q", "k", "g"):
                src, _ = wsrc[pi]
                for h in heads:
                    if nw >= 2:
                        P.wait("gpsimd", t4, nw - 1)
                    P.do("gpsimd", lambda e, src=src, h=h, nw=nw: e.dma_start(
                        out=wslv[nw % 2], in_=src[h]),
                        wsem[nw % 2], 16)
                    nw += 1
            iw = g4 * 12
            for pi in ("q", "k", "g"):
                src, bbase = wsrc[pi]
                for hh, h in enumerate(heads):
                    par = iw % 2
                    P.wait("tensor", wsem[par],
                           wbase[par] + (iw // 2 + 1) * 16)
                    if iw >= 1:
                        P.wait("tensor", S["s4p"], iw)

                    def pmm(e, par=par):
                        ins = None
                        for th in range(2):
                            for c in range(NCH):
                                ins = e.matmul(
                                    out=psP[:, th * 512:(th + 1) * 512],
                                    lhsT=wslv[par][:, c, :],
                                    rhs=xv4[:, c, th * 512:(th + 1) * 512],
                                    start=(c == 0), stop=(c == NCH - 1))
                        return ins
                    P.do("tensor", pmm, t4)
                    P.wait("scalar", t4, t4.n)
                    bcol = bbase + h
                    if pi == "g":
                        if g4 > 0:
                            P.wait("scalar", S["s42"], 16 * g4)
                        P.do("scalar", lambda e, hh=hh, bcol=bcol:
                             e.activation(
                                 out=gb4[:, hh * SL:(hh + 1) * SL],
                                 in_=psP[:], func=ACT.Sigmoid,
                                 bias=b1sb[:, bcol:bcol + 1]), S["s4p"])
                    else:
                        if iw >= 1:
                            P.wait("scalar", v4, v4.n)
                        P.do("scalar", lambda e, bcol=bcol: e.activation(
                            out=A4[:], in_=psP[:], func=ACT.Identity,
                            bias=b1sb[:, bcol:bcol + 1]), S["s4p"])
                        P.wait("vector", S["s4p"], S["s4p"].n)
                        dstb = qb4 if pi == "q" else kb4
                        spillsem = S["o40"] if pi == "q" else S["o41"]
                        if g4 > 0:
                            P.wait("vector", spillsem, 16 * g4)
                        P.do("vector", lambda e: e.stream_shuffle(
                            out=As4[:], in_=A4[:], mask=shufmask))
                        P.do("vector", lambda e: e.tensor_mul(
                            out=A4[:], in0=A4[:], in1=ropeCb[:]))
                        P.do("vector", lambda e: e.tensor_mul(
                            out=As4[:], in0=As4[:], in1=ropeSb[:]))
                        P.do("vector", lambda e: e.tensor_add(
                            out=A4[:], in0=A4[:], in1=As4[:]))
                        P.do("vector", lambda e: e.tensor_single_scalar(
                            out=t4a[:], in_=A4[:], scalar=0.0, op=ALU.min),
                            v4)
                        P.wait("scalar", v4, v4.n)
                        P.do("scalar", lambda e: e.activation(
                            out=t4a[:], in_=t4a[:], func=ACT.Exp), s4e)
                        P.wait("vector", s4e, s4e.n)
                        P.do("vector", lambda e, dstb=dstb, hh=hh:
                             e.scalar_tensor_tensor(
                                 out=dstb[:, hh * SL:(hh + 1) * SL],
                                 in0=A4[:], scalar=0.0, in1=t4a[:],
                                 op0=ALU.max, op1=ALU.add), v4)
                    iw += 1
            P.wait("tensor", S["g4v"], 16 * (g4 + 1))
            for tt in range(8):
                if tv.n > tv_b4:
                    P.wait("tensor", vv, vv_b4 + (tv.n - tv_b4))

                def vmm4(e, tt=tt):
                    ins = None
                    for c in range(NCH):
                        ins = e.matmul(
                            out=psV4[:],
                            lhsT=xv4[:, c, tt * 128:(tt + 1) * 128],
                            rhs=wvg[:, c * 512:(c + 1) * 512],
                            start=(c == 0), stop=(c == NCH - 1))
                    return ins
                P.do("tensor", vmm4, tv)
                P.wait("vector", tv, tv.n)
                P.do("vector", lambda e, tt=tt, g4=g4:
                     e.scalar_tensor_tensor(
                         out=vgrp4[:, g4 * 4096 + tt * 512:
                                   g4 * 4096 + (tt + 1) * 512], in0=psV4[:],
                         scalar=1.0,
                         in1=bv1sb[:, g4 * 512:(g4 + 1) * 512],
                         op0=ALU.mult, op1=ALU.add), vv)
            for hh in range(4):
                P.wait("tensor", v4, v4.n)
                for tt in range(8):
                    if t4t.n > 0:
                        P.wait("tensor", v4b, t4t.n)
                    P.do("tensor", lambda e, hh=hh, tt=tt: e.transpose(
                        out=psT4[:],
                        in_=kb4[:, hh * SL + tt * 128:hh * SL + (tt + 1) * 128],
                        identity=eyeb[:]), t4t)
                    P.wait("vector", t4t, t4t.n)
                    P.do("vector", lambda e, hh=hh, tt=tt: e.tensor_copy(
                        out=ktm4[:, hh * SL + tt * 128:
                                 hh * SL + (tt + 1) * 128],
                        in_=psT4[:]), v4b)
            P.wait("tensor", vv, vv.n)
            for hh, h in enumerate(heads):
                P.wait("tensor", v4b, v4b.n)

                def smm(e, hh=hh, g4=g4):
                    ins = None
                    for tt in range(8):
                        kt_ = ktm4[:, hh * SL + tt * 128:
                                   hh * SL + (tt + 1) * 128]
                        e.matmul(out=psSZ[:, 0:128], lhsT=kt_,
                                 rhs=vgrp4[:, g4 * 4096 + tt * 512 + hh * 128:
                                           g4 * 4096 + tt * 512 +
                                           (hh + 1) * 128],
                                 start=(tt == 0), stop=(tt == 7))
                    for tt in range(8):
                        kt_ = ktm4[:, hh * SL + tt * 128:
                                   hh * SL + (tt + 1) * 128]
                        ins = e.matmul(out=psSZ[:, 128:129], lhsT=kt_,
                                       rhs=onesc[:],
                                       start=(tt == 0), stop=(tt == 7))
                    return ins
                P.do("tensor", smm, t4t)
                P.wait("vector", t4t, t4t.n)
                P.do("vector", lambda e, h=h: e.tensor_copy(
                    out=sloc[:, h * 132:h * 132 + 129], in_=psSZ[:]), v4b)
            P.wait("sync", v4, v4.n)
            P.wait("sync", vv, vv.n)
            P.wait("sync", S["s4p"], S["s4p"].n)
            P.do("sync", lambda e, g4=g4: e.dma_start(
                out=q1d[g4],
                in_=qb4.rearrange("p (hh t) -> p hh t", hh=4)), S["o40"], 16)
            P.do("sync", lambda e, g4=g4: e.dma_start(
                out=k1d[g4],
                in_=kb4.rearrange("p (hh t) -> p hh t", hh=4)), S["o41"], 16)
            P.do("sync", lambda e, g4=g4: e.dma_start(
                out=g1d[g4],
                in_=gb4.rearrange("p (hh t) -> p hh t", hh=4)), S["s42"], 16)

        P.wait("sync", v4b, v4b.n)
        P.do("sync", lambda e: e.dma_start(
            out=cin1.rearrange("(h p) w -> p h w", p=128),
            in_=sloc.rearrange("p (h w) -> p h w", h=NCH)), S["s43"], 16)
        P.wait("gpsimd", S["s43"], S["s43"].n)
        P.do("gpsimd", lambda e: e.collective_compute(
            "AllGather", ALU.bypass, ins=[cin1[:]], outs=[cout1[:]],
            replica_groups=groups), cc)

        s4A.close()
        phase_barrier()

        o1 = sbuf(s4, "o1", [128, NCH * SL], BF16)
        s4B = ExitStack()
        sing = sbuf(s4B, "sing", [128, NCH * 132])
        sgt = sbuf(s4B, "sgt", [128, NCH * 132])
        sinb = sbuf(s4B, "sinb", [128, NCH * 132], BF16)
        m1sb = sbuf(s4B, "m1sb", [128, 4 * 512], BF16)
        qh = sbuf(s4B, "qh", [128, 2 * SL], BF16)
        kh = sbuf(s4B, "kh", [128, 2 * SL], BF16)
        gh = sbuf(s4B, "gh", [128, 2 * SL], BF16)
        ghf = sbuf(s4B, "ghf", [128, SL])
        pb4 = sbuf(s4B, "pb4", [128, 2 * 512], BF16)
        t4c = sbuf(s4B, "t4c", [128, 512])
        psO = psum(s4B, "psO", [128, 512])
        psS4 = psum(s4B, "psS4", [128, 512])
        psS4b = psum(s4B, "psS4b", [128, 512])
        psBC4 = psum(s4B, "psBC4", [128, 512])
        psDen4 = psum(s4B, "psDen4", [1, 512])

        P.do("sync", lambda e: e.dma_start(out=m1sb[:], in_=masks1[:]),
             sld, 16)
        P.do("vector", lambda e: e.memset(sing[:], 0.0), v4)
        cout1v = cout1.rearrange("(b h p) w -> b h p w", b=4, p=128)
        P.wait("gpsimd", cc, cc.n)
        for pp in range(4):
            if pp >= 1:
                P.wait("gpsimd", v4, v4.n)
            P.do("gpsimd", lambda e, pp=pp: e.dma_start(
                out=sgt.rearrange("p (h w) -> p h w", h=NCH),
                in_=cout1v[pp].transpose([1, 0, 2])), gd, 16)
            P.wait("vector", gd, gd.n)
            P.do("vector", lambda e, pp=pp: e.scalar_tensor_tensor(
                out=sing[:], in0=sgt[:], scalar=pmsb[:, pp:pp + 1],
                in1=sing[:], op0=ALU.mult, op1=ALU.add), v4)
        P.do("vector", lambda e: e.tensor_copy(out=sinb[:], in_=sing[:]), v4)
        P.wait("tensor", v4, v4.n)
        P.wait("tensor", sld, sld.n)
        P.wait("vector", sld, sld.n)

        hsem = [S["s40"], S["s41"]]
        hbase = [S["s40"].n, S["s41"].n]
        tsc_b4 = tsc.n
        vm_b4 = vm.n
        ta_b4 = ta.n
        nheads_done = 0
        for h in range(NCH):
            par = h % 2
            hsl = slice(par * SL, (par + 1) * SL)
            if h >= 2:
                P.wait("sync", t4t, t4t.n)
            for dst, srcd in ((qh, q1d), (kh, k1d), (gh, g1d)):
                P.do("sync", lambda e, dst=dst, srcd=srcd, h=h, hsl=hsl:
                     e.dma_start(out=dst[:, hsl],
                                 in_=srcd[h // 4][:, h % 4, :]),
                     hsem[par], 16)
            P.wait("tensor", hsem[par], hbase[par] + (h // 2 + 1) * 48)
            P.wait("vector", hsem[par], hbase[par] + (h // 2 + 1) * 48)
            P.do("vector", lambda e, hsl=hsl: e.tensor_copy(
                out=ghf[:], in_=gh[:, hsl]), v4)
            for qt in range(2):
                qsl = slice(par * SL + qt * 512, par * SL + (qt + 1) * 512)
                if S["v4"].n > 0:
                    P.wait("tensor", S["v4"], S["v4"].n)
                P.do("tensor", lambda e, h=h, qsl=qsl: e.matmul(
                    out=psO[:], lhsT=sinb[:, h * 132:h * 132 + 128],
                    rhs=qh[:, qsl], start=True, stop=False))
                P.do("tensor", lambda e, h=h, qsl=qsl: e.matmul(
                    out=psDen4[:], lhsT=sinb[:, h * 132 + 128:h * 132 + 129],
                    rhs=qh[:, qsl], start=True, stop=False))
                nkt = 4 * qt + 4
                g0 = tsc.n - tsc_b4

                def emit_score4(kt, qsl=qsl, par=par, qt=qt, g0=g0):
                    g = g0 + kt
                    psSp = psS4 if g % 2 == 0 else psS4b
                    pbs = pb4[:, (g % 2) * 512:(g % 2 + 1) * 512]
                    if g > 1:
                        P.wait("tensor", vm, vm_b4 + g - 1)
                    P.do("tensor", lambda e, kt=kt, qsl=qsl, par=par,
                         psSp=psSp:
                         e.matmul(out=psSp[:],
                                  lhsT=kh[:, par * SL + kt * 128:
                                          par * SL + (kt + 1) * 128],
                                  rhs=qh[:, qsl], start=True, stop=True), tsc)
                    P.wait("vector", tsc, tsc.n)
                    if g > 1:
                        P.wait("vector", ta, ta_b4 + g - 1)
                    if kt >= 4 * qt:
                        j = kt - 4 * qt
                        P.do("vector", lambda e, j=j, psSp=psSp, pbs=pbs:
                             e.tensor_mul(
                                 out=pbs, in0=psSp[:],
                                 in1=m1sb[:, j * 512:(j + 1) * 512]), vm)
                    else:
                        P.do("vector", lambda e, psSp=psSp, pbs=pbs:
                             e.tensor_copy(out=pbs, in_=psSp[:]), vm)

                emit_score4(0)
                for kt in range(nkt):
                    g = g0 + kt
                    pbs = pb4[:, (g % 2) * 512:(g % 2 + 1) * 512]
                    if kt + 1 < nkt:
                        emit_score4(kt + 1)
                    P.wait("tensor", vm, vm_b4 + g + 1)
                    P.do("tensor", lambda e, kt=kt, h=h, nkt=nkt, pbs=pbs:
                         e.matmul(out=psO[:],
                                  lhsT=vgrp4[:, (h // 4) * 4096 + kt * 512 +
                                             (h % 4) * 128:
                                             (h // 4) * 4096 + kt * 512 +
                                             (h % 4) * 128 + 128],
                                  rhs=pbs, start=False,
                                  stop=(kt == nkt - 1)))
                    P.do("tensor", lambda e, kt=kt, nkt=nkt, pbs=pbs: e.matmul(
                        out=psDen4[:], lhsT=onesc[:], rhs=pbs,
                        start=False, stop=(kt == nkt - 1)), ta)
                P.wait("vector", ta, ta.n)
                P.do("vector", lambda e, qt=qt: e.tensor_mul(
                    out=t4c[:], in0=psO[:],
                    in1=ghf[:, qt * 512:(qt + 1) * 512]), S["v4"])
                P.wait("vector", tb, tb.n)
                P.do("vector", lambda e: e.tensor_scalar_add(
                    out=rows[:, 0:512], in0=psDen4[:], scalar1=1e-6))
                P.do("vector", lambda e: e.reciprocal(
                    out=rows[:, 512:1024], in_=rows[:, 0:512]), vr)
                P.wait("tensor", vr, vr.n)
                if vo.n > 0:
                    P.wait("tensor", vo, vo.n)
                P.do("tensor", lambda e: e.matmul(
                    out=psBC4[:], lhsT=onesrF[:], rhs=rows[:, 512:1024],
                    start=True, stop=True), tb)
                P.wait("vector", tb, tb.n)
                P.do("vector", lambda e, h=h, qt=qt: e.tensor_mul(
                    out=o1[:, h * SL + qt * 512:h * SL + (qt + 1) * 512],
                    in0=t4c[:], in1=psBC4[:]), vo)
            P.wait("tensor", vo, vo.n)
            P.do("tensor", lambda e: e.drain(), t4t)

        s4B.close()
        phase_barrier()

        s4C = ExitStack()
        strm4 = sbuf(s4C, "strm4", [128, NCH * SL])
        lnb4 = sbuf(s4C, "lnb4", [128, 2 * SL], BF16)
        wosb4 = sbuf(s4C, "wosb4", [128, 2 * NCH * 128], BF16)
        psA4 = psum(s4C, "psA4", [128, 512])
        psB4 = psum(s4C, "psB4", [128, 512])
        psR4 = [psum(s4C, f"psR4{i}", [1, 512]) for i in range(4)]

        P.do("sync", lambda e: e.dma_start(
            out=strm4.rearrange("p (c t) -> p c t", c=NCH),
            in_=strm_d[:]), sld, 16)
        o1v = o1.rearrange("p (c t) -> p c t", c=NCH)
        tx_b4 = tx.n
        wo_base4 = [wo_s[0].n, wo_s[1].n]
        for c in range(NCH):
            if c >= 2:
                P.wait("gpsimd", tx, tx_b4 + 2 * (c - 1))
            P.do("gpsimd", lambda e, c=c: e.dma_start(
                out=wosb4[:, (c % 2) * 2048:(c % 2 + 1) * 2048]
                .rearrange("p (c2 f) -> p c2 f", c2=NCH),
                in_=wo1t[c]), wo_s[c % 2], 16)
        vx_b4 = vx.n
        P.wait("vector", sld, sld.n)
        for c in range(NCH):
            P.wait("tensor", wo_s[c % 2], wo_base4[c % 2] + (c // 2 + 1) * 16)
            for th in range(2):
                pq = psA4 if th == 0 else psB4
                if c >= 1:
                    P.wait("tensor", vx, vx_b4 + 2 * (c - 1) + th + 1)

                def womm4(e, c=c, th=th, pq=pq):
                    w = wosb4[:, (c % 2) * 2048:(c % 2 + 1) * 2048].rearrange(
                        "p (c2 f) -> p c2 f", c2=NCH)
                    ins = None
                    for ech in range(NCH):
                        ins = e.matmul(
                            out=pq[:], lhsT=w[:, ech, :],
                            rhs=o1v[:, ech, th * 512:(th + 1) * 512],
                            start=(ech == 0), stop=(ech == NCH - 1))
                    return ins
                P.do("tensor", womm4, tx)
                P.wait("vector", tx, tx.n)
                P.do("vector", lambda e, c=c, th=th, pq=pq:
                     e.scalar_tensor_tensor(
                         out=strm4[:, c * SL + th * 512:
                                   c * SL + (th + 1) * 512],
                         in0=pq[:], scalar=bo1sb[:, c:c + 1],
                         in1=strm4[:, c * SL + th * 512:
                                   c * SL + (th + 1) * 512],
                         op0=ALU.add, op1=ALU.add), vx)
        emit_layernorm(psA4, psB4, psR4, lnb4, strm4)

        if stop_after == 4:
            emit_output(strm4)
            s4C.close()
            s4.close()
            P.emit(block)
            return nc

        P.wait("sync", vx, vx.n)
        P.do("sync", lambda e: e.dma_start(
            out=strm_d[:],
            in_=strm4.rearrange("p (c t) -> p c t", c=NCH)), S["sdw"], 16)
        phase_barrier()
        s4C.close()
        s4.close()

        s5 = ExitStack()
        fuseb = sbuf(s5, "fuseb", [128, NICH * SL], BF16)
        s5a = ExitStack()
        xb5 = sbuf(s5a, "xb5", [128, NCH * SL], BF16)
        g8 = sbuf(s5a, "g8", [128, 4 * D], FP8)
        u8 = sbuf(s5a, "u8", [128, 4 * D], FP8)
        t5g = sbuf(s5a, "t5g", [128, 2 * SL], BF16)
        t5u = sbuf(s5a, "t5u", [128, 2 * SL], BF16)
        psG = [psum(s5a, f"psG{i}", [128, SL]) for i in range(2)]
        psU = [psum(s5a, f"psU{i}", [128, SL]) for i in range(2)]

        P.do("gpsimd", lambda e: e.dma_start(
            out=xb5.rearrange("p (c t) -> p c t", c=NCH),
            in_=strm_d[:]), gd, 16)
        for eng in ("tensor", "vector", "scalar"):
            P.wait(eng, gd)
        xv5 = xb5.rearrange("p (c t) -> p c t", c=NCH)
        g8v = [g8[:, i * D:(i + 1) * D].rearrange("p (c f) -> p c f", c=NCH)
               for i in range(4)]
        u8v = [u8[:, i * D:(i + 1) * D].rearrange("p (c f) -> p c f", c=NCH)
               for i in range(4)]
        gsem = S["g50"]
        gsem_base = gsem.n
        t5 = S["t5"]
        t5_b5 = t5.n
        s4e_b5 = s4e.n
        v4_b5 = S["v4"].n
        v4b_b5 = v4b.n
        for ich in range(NICH):
            slot = ich % 4
            par = ich % 2
            if ich >= 4:
                P.wait("gpsimd", t5, t5_b5 + ich - 3)
            P.do("gpsimd", lambda e, ich=ich, slot=slot: e.dma_start(
                out=g8v[slot], in_=gw1t[ich]), gsem, 16)
            P.do("gpsimd", lambda e, ich=ich, slot=slot: e.dma_start(
                out=u8v[slot], in_=uw1t[ich]), gsem, 16)
            P.wait("tensor", gsem, gsem_base + 32 * (ich + 1))
            if ich >= 2:
                P.wait("tensor", s4e, s4e_b5 + ich - 1)
                P.wait("tensor", S["v4"], v4_b5 + ich - 1)

            def gumm(e, slot=slot, par=par):
                ins = None
                for th in range(2):
                    for c in range(NCH):
                        e.matmul(out=psG[par][:, th * 512:(th + 1) * 512],
                                 lhsT=g8v[slot][:, c, :],
                                 rhs=xv5[:, c, th * 512:(th + 1) * 512],
                                 start=(c == 0), stop=(c == NCH - 1))
                for th in range(2):
                    for c in range(NCH):
                        ins = e.matmul(out=psU[par][:, th * 512:(th + 1) * 512],
                                       lhsT=u8v[slot][:, c, :],
                                       rhs=xv5[:, c, th * 512:(th + 1) * 512],
                                       start=(c == 0), stop=(c == NCH - 1))
                return ins
            P.do("tensor", gumm, t5)
            P.wait("scalar", t5, t5.n)
            if ich >= 2:
                P.wait("scalar", v4b, v4b_b5 + ich - 1)
            P.do("scalar", lambda e, ich=ich, par=par: e.activation(
                out=t5g[:, par * SL:(par + 1) * SL], in_=psG[par][:],
                func=ACT.Silu, scale=gscsb[:, ich:ich + 1]), s4e)
            P.wait("vector", t5, t5.n)
            P.do("vector", lambda e, ich=ich, par=par: e.tensor_scalar(
                out=t5u[:, par * SL:(par + 1) * SL], in0=psU[par][:],
                scalar1=uscsb[:, ich:ich + 1],
                scalar2=None, op0=ALU.mult), S["v4"])
            P.wait("vector", s4e, s4e.n)
            P.do("vector", lambda e, ich=ich, par=par: e.tensor_mul(
                out=fuseb[:, ich * SL:(ich + 1) * SL],
                in0=t5g[:, par * SL:(par + 1) * SL],
                in1=t5u[:, par * SL:(par + 1) * SL]), v4b)

        s5a.close()
        phase_barrier()

        s5b = ExitStack()
        strm5 = sbuf(s5b, "strm5", [128, NCH * SL])
        lnb5 = sbuf(s5b, "lnb5", [128, 2 * SL], BF16)
        d8 = sbuf(s5b, "d8", [128, 2 * FI], FP8)
        psA5 = psum(s5b, "psA5", [128, 512])
        psB5 = psum(s5b, "psB5", [128, 512])
        psR5 = [psum(s5b, f"psR5{i}", [1, 512]) for i in range(4)]

        P.do("sync", lambda e: e.dma_start(
            out=strm5.rearrange("p (c t) -> p c t", c=NCH),
            in_=strm_d[:]), sld, 16)
        d8v = [d8[:, i * FI:(i + 1) * FI]
               .rearrange("p (c f) -> p c f", c=NICH) for i in range(2)]
        dsem = S["d50"]
        dsem_base = dsem.n
        fv = fuseb.rearrange("p (c t) -> p c t", c=NICH)
        P.wait("vector", sld, sld.n)
        t5b_base = t5.n
        vx_dn = vx.n
        nfill = 0
        for cch in range(NCH):
            dpar = cch % 2
            if cch >= 2:
                P.wait("gpsimd", t5, t5b_base + 2 * (cch - 1))
            P.do("gpsimd", lambda e, cch=cch, dpar=dpar: e.dma_start(
                out=d8v[dpar], in_=dw1t[cch]), dsem, 16)
            P.wait("tensor", dsem, dsem_base + 16 * (cch + 1))
            for th5 in range(2):
                pq = psA5 if nfill % 2 == 0 else psB5
                if nfill >= 2:
                    P.wait("tensor", vx, vx_dn + nfill - 1)

                def dmm(e, dpar=dpar, pq=pq, th5=th5):
                    ins = None
                    for c in range(NICH):
                        ins = e.matmul(out=pq[:], lhsT=d8v[dpar][:, c, :],
                                       rhs=fv[:, c, th5 * 512:
                                              (th5 + 1) * 512],
                                       start=(c == 0), stop=(c == NICH - 1))
                    return ins
                P.do("tensor", dmm, t5)
                P.wait("vector", t5, t5.n)
                P.do("vector", lambda e, cch=cch, th5=th5, pq=pq:
                     e.scalar_tensor_tensor(
                         out=strm5[:, cch * SL + th5 * 512:
                                   cch * SL + (th5 + 1) * 512],
                         in0=pq[:], scalar=dscsb[:, cch:cch + 1],
                         in1=strm5[:, cch * SL + th5 * 512:
                                   cch * SL + (th5 + 1) * 512],
                         op0=ALU.mult, op1=ALU.add), vx)
                nfill += 1
        emit_layernorm(psA5, psB5, psR5, lnb5, strm5)
        emit_output(strm5)
        s5b.close()
        s5.close()
        P.emit(block)
    return nc


def _tile_k(wt, f_slice=None):
    a = wt if f_slice is None else wt[:, f_slice]
    K, F = a.shape
    return np.ascontiguousarray(a.reshape(K // 128, 128, F).transpose(1, 0, 2))


def _tile_kf(wt):
    K, F = wt.shape
    a = wt.reshape(K // 128, 128, F // 128, 128)
    return np.ascontiguousarray(a.transpose(2, 1, 0, 3))


def _part_major(vec):
    n = vec.shape[0] // 128
    return np.ascontiguousarray(vec.reshape(n, 128).T)


def kernel(**inp):
    f32 = np.float32
    X = np.asarray(inp["L0_attn_X"], f32)
    out = np.empty((2, TB, D), f32)

    wo0t = _tile_kf(np.ascontiguousarray(np.asarray(inp["L0_attn_Wo"], f32).T))
    fk0 = np.ascontiguousarray(np.asarray(inp["L0_ffn_kernel"], f32)
                               .reshape(NCH, 128, D))
    fb0 = _part_major(np.asarray(inp["L0_ffn_bias"], f32))
    bo0 = _part_major(np.asarray(inp["L0_attn_bo"], f32))
    w1 = {nm: _tile_kf(np.ascontiguousarray(
        np.asarray(inp[f"L1_attn_{nm}"], f32).T))
        for nm in ["Wq", "Wk", "Wg"]}
    wv1 = np.asarray(inp["L1_attn_Wv"], f32).T
    wv1t = np.ascontiguousarray(
        wv1.reshape(NCH, 128, 4, 512).transpose(2, 1, 0, 3))
    wo1t = _tile_kf(np.ascontiguousarray(np.asarray(inp["L1_attn_Wo"], f32).T))
    b1 = np.ascontiguousarray(np.concatenate([
        _part_major(np.asarray(inp["L1_attn_bq"], f32)),
        _part_major(np.asarray(inp["L1_attn_bk"], f32)),
        _part_major(np.asarray(inp["L1_attn_bg"], f32))], axis=1))
    bv1b = np.ascontiguousarray(np.broadcast_to(
        np.asarray(inp["L1_attn_bv"], f32)[None, :], (128, D)))
    bo1 = _part_major(np.asarray(inp["L1_attn_bo"], f32))
    gw1t = _tile_kf(np.asarray(inp["L1_ffn_gate_w"], f32))
    uw1t = _tile_kf(np.asarray(inp["L1_ffn_up_w"], f32))
    dw1t = _tile_kf(np.asarray(inp["L1_ffn_down_w"], f32))
    gsc = _part_major(np.asarray(inp["L1_ffn_gate_scale"], f32))
    usc = _part_major(np.asarray(inp["L1_ffn_up_scale"], f32))
    dsc = _part_major(np.asarray(inp["L1_ffn_down_scale"], f32))

    k_idx = np.arange(128)[:, None]
    q_idx = np.arange(512)[None, :]
    m0 = np.concatenate(
        [((128 * j + k_idx) <= q_idx).astype(BF) for j in range(4)], axis=1)
    m1 = np.concatenate(
        [(((128 * j + k_idx) // 64) <= (q_idx // 64)).astype(BF)
         for j in range(4)], axis=1)
    eye = np.eye(128, dtype=BF)
    onesr = np.ones((1, 128), dtype=BF)

    inv = (10000.0 ** (-np.arange(0, 128, 2, dtype=np.float64) / 128.0))
    pos = np.arange(TB, dtype=np.float64)
    ang = pos[:, None] * inv[None, :]
    cosv, sinv = np.cos(ang), np.sin(ang)
    Cfull = np.empty((128, TB), f32)
    Sfull = np.empty((128, TB), f32)
    Cfull[0::2, :] = cosv.T
    Cfull[1::2, :] = cosv.T
    Sfull[0::2, :] = -sinv.T
    Sfull[1::2, :] = sinv.T

    wq0T = np.asarray(inp["L0_attn_Wq"], f32).T
    wk0T = np.asarray(inp["L0_attn_Wk"], f32).T
    wv0T = np.asarray(inp["L0_attn_Wv"], f32).T
    bq0 = np.asarray(inp["L0_attn_bq"], f32)
    bk0 = np.asarray(inp["L0_attn_bk"], f32)
    bv0 = np.asarray(inp["L0_attn_bv"], f32)

    in_maps = []
    XT = [np.ascontiguousarray(X[b].T).reshape(NCH, 128, TB) for b in range(2)]
    XTtile = [np.ascontiguousarray(
        XT[b].reshape(NCH, 128, 8, 512).transpose(2, 1, 0, 3))
        for b in range(2)]
    for c in range(8):
        b, p = c // 4, c % 4
        fsl = slice(512 * p, 512 * (p + 1))
        b0c = np.ascontiguousarray(np.concatenate(
            [bq0[fsl].reshape(4, 128), bk0[fsl].reshape(4, 128),
             bv0[fsl].reshape(4, 128)], axis=0).T)
        m = {
            "xtf": XTtile[b],
            "xown": np.ascontiguousarray(
                XT[b][:, :, p * SL:(p + 1) * SL].transpose(1, 0, 2)),
            "wq0t": _tile_k(wq0T, fsl), "wk0t": _tile_k(wk0T, fsl),
            "wv0t": _tile_k(wv0T, fsl),
            "b0": b0c,
            "bv0b": np.ascontiguousarray(
                np.broadcast_to(bv0[fsl][None, :], (128, 512))),
            "wo0t": wo0t, "bo0": bo0, "fk0": fk0, "fb0": fb0,
            "masks0": m0, "masks1": m1,
            "ropeC": np.ascontiguousarray(Cfull[:, p * SL:(p + 1) * SL]),
            "ropeS": np.ascontiguousarray(Sfull[:, p * SL:(p + 1) * SL]),
            "wq1t": w1["Wq"], "wk1t": w1["Wk"], "wg1t": w1["Wg"],
            "wv1t": wv1t, "b1qkg": b1, "bv1b": bv1b,
            "wo1t": wo1t, "bo1": bo1,
            "gw1t": gw1t, "uw1t": uw1t, "dw1t": dw1t,
            "gsc": gsc, "usc": usc, "dsc": dsc,
            "pidx": np.array([[p]], np.int32),
            "pm": np.ascontiguousarray(np.broadcast_to(
                (np.arange(4) < p).astype(f32)[None, :], (128, 4))),
            "eyein": eye, "onesrin": onesr,
        }
        in_maps.append(m)

    nc = build_program(STOP_AFTER)
    res = run_bass_kernel_spmd(nc, in_maps, core_ids=list(range(8)))
    for c in range(8):
        b, p = c // 4, c % 4
        out[b, p * SL:(p + 1) * SL, :] = res.results[c]["y"].T
    return out
